# version 15 (final)
# speedup vs baseline: 1.3101x; 1.0301x over previous
import numpy as np
import ml_dtypes
import concourse.bass as bass
import concourse.mybir as mybir
from contextlib import ExitStack

F32 = mybir.dt.float32
BF16 = mybir.dt.bfloat16
I16 = mybir.dt.int16
I32 = mybir.dt.int32
U8 = mybir.dt.uint8
F8 = mybir.dt.float8e4
AF = mybir.ActivationFunctionType
OP = mybir.AluOpType
AX = mybir.AxisListType.X
PI = float(np.pi)

D = 2048
NCH = 16
NK = 1024
HALO = 3
WARM = 125
SCAN = WARM + HALO + NK
XR = SCAN + 1
KEPT = HALO + NK
TILES = [(0, 512), (512, 512), (1024, 128)]
KTILES = [(125, 387, 0), (512, 512, 387), (1024, 128, 899)]
LKT = [(0, 512), (512, 512)]
HID = 5632
NHC = 44
FLAT = 2048
NCORES = 8
TOTC = 34
NCONST = 12
C_PIHALF, C_INV2PI, C_EPS5, C_N1, C_EPS30, C_ONE, C_NHALF = 0, 1, 2, 3, 4, 5, 6

ENGS = ("T", "V", "A", "G", "S")


class Bld:
    def __init__(self, nc, es):
        self.nc = nc
        self.es = es
        self.L = {e: [] for e in ENGS}
        self.sems = {e: es.enter_context(nc.semaphore("sem" + e)) for e in ENGS}
        self.cnt = {e: 0 for e in ENGS}
        self.dsems = {}
        self.dcnt = {}
        self.phase = "init"

    def op(self, eng, fn, ev=False):
        if ev:
            self.cnt[eng] += 1
            c = self.cnt[eng]
            sem = self.sems[eng]
            self.L[eng].append((self.phase,
                                lambda h, fn=fn, sem=sem: fn(h).then_inc(sem, 1)))
            return (eng, c)
        self.L[eng].append((self.phase, fn))
        return None

    def wait(self, eng, event):
        if event is None:
            return
        seng, c = event
        if c <= 0:
            return
        sem = self.sems[seng] if seng in self.sems else self.dsems[seng]
        self.L[eng].append((self.phase,
                            lambda h, sem=sem, c=c: h.wait_ge(sem, c)))

    def dma(self, eng, slot, out_ap, in_ap, free=None):
        key = "d:" + slot
        if key not in self.dsems:
            self.dsems[key] = self.es.enter_context(self.nc.semaphore(key))
            self.dcnt[key] = 0
        if free is not None:
            self.wait(eng, free)
        self.dcnt[key] += 16
        sem = self.dsems[key]
        self.L[eng].append((self.phase,
                            lambda h, o=out_ap, i=in_ap, sem=sem: h.dma_start(
                                out=o, in_=i).then_inc(sem, 16)))
        return (key, self.dcnt[key])

    def barrier(self, engs=("T", "V", "A"), consumers=("T", "V", "A", "S", "G")):
        evs = [(e, self.cnt[e]) for e in engs]
        for c in consumers:
            for ev in evs:
                if ev[0] != c and ev[1] > 0:
                    self.wait(c, ev)

    def emit(self, block):
        nc = self.nc
        for eng, deco in [("S", block.sync), ("T", block.tensor), ("V", block.vector),
                          ("A", block.scalar), ("G", block.gpsimd)]:
            ops = self.L[eng]
            if not ops:
                continue
            def section(h, ops=ops):
                i = 0
                n = len(ops)
                while i < n:
                    ph = ops[i][0]
                    j = i
                    with nc.named_scope(ph):
                        while j < n and ops[j][0] == ph:
                            ops[j][1](h)
                            j += 1
                    i = j
            deco(section)


def build(debug=None):
    nc = bass.Bass("TRN2", target_bir_lowering=False, detect_race_conditions=False)
    es = ExitStack()
    E = es.enter_context
    b = Bld(nc, es)
    open_stacks = []

    def mkstack():
        st = ExitStack()
        open_stacks.append(st)
        return st

    def closestack(st):
        assert st in open_stacks
        open_stacks.remove(st)
        st.close()

    def close_all():
        for st in reversed(open_stacks):
            st.close()
        open_stacks.clear()
    bfree = {i: None for i in range(8)}

    def din(name, shape, dt):
        return nc.dram_tensor(name, shape, dt, kind="ExternalInput").ap()

    xT = din("xT", [D, XR], F32)
    maskb = din("maskb", [1, SCAN], BF16)
    wkE = din("wkE", [NCH, 128, D], F32)
    wvE = din("wvE", [NCH, 128, D], F32)
    wrE = din("wrE", [NCH, 128, D], F32)
    woE = din("woE", [NCH, 128, D], F32)
    f8E = din("f8E", [NCH, 128, D], F8)
    tm_in = din("tm_in", [128, 3 * NCH], F32)
    td_in = din("td_in", [128, NCH], F32)
    tf_in = din("tf_in", [128, NCH], F32)
    scl_in = din("scl_in", [1, 2], F32)
    consts = din("consts", [128, NCONST], F32)
    ones_bf_in = din("ones_bf_in", [1, 128], BF16)
    ones_f_in = din("ones_f_in", [1, 128], F32)
    ones_col_in = din("ones_col_in", [128, 1], BF16)
    onesT_in = din("onesT_in", [128, 512], F32)
    wmem_in = din("wmem_in", [128, NCH * 384], F32)
    wqp_in = din("wqp_in", [128, NCH * 512], F32)
    ck_in = din("ck_in", [128, 12], F32)
    theta_in = din("theta_in", [128, NCH], F32)
    sslop_in = din("sslop_in", [32, 1], F32)
    ssc_in = din("ssc_in", [32, 1], F32)
    feat_in = din("feat_in", [32, NK], F32)
    repkh_in = din("repkh_in", [2, 128, FLAT], F32)
    repk_in = din("repk_in", [32, FLAT], F32)
    repden_in = din("repden_in", [32, 256], F32)
    repqre_in = din("repqre_in", [4, 128, FLAT], BF16)
    repqim_in = din("repqim_in", [4, 128, FLAT], BF16)
    sum8_in = din("sum8_in", [NCH, 128, 256], BF16)
    wblkre_in = din("wblkre_in", [2, 128, 1536], F32)
    wblkim_in = din("wblkim_in", [2, 128, 1536], F32)
    normc_in = din("normc_in", [128, 2], F32)
    woutE = din("woutE", [6, 128, D], F32)
    wgE = din("wgE", [NHC, 128, D], U8)
    wuE = din("wuE", [NHC, 128, D], U8)
    wdE = din("wdE", [NCH, 128, HID], U8)
    sgu_in = din("sgu_in", [128, 2 * NHC + NCH], F32)
    mcon_in = din("mcon_in", [128, NCORES], F32)
    msel_in = din("msel_in", [128, NCORES], F32)

    out = nc.dram_tensor("out", [D, NK], F32, kind="ExternalOutput").ap()
    dbg = nc.dram_tensor("dbg", [D, KEPT], F32, kind="ExternalOutput").ap() \
        if debug in ("x1", "x2", "x3", "y") else None
    dbg16 = nc.dram_tensor("dbg16", [128, NCH * SCAN], BF16, kind="ExternalOutput").ap() \
        if debug in ("xk", "xv", "xr", "xr0", "sr", "wkvsr") else None
    dbgF = nc.dram_tensor("dbgF", [128, 3074], F32, kind="ExternalOutput").ap() \
        if debug == "a3i" else None
    dbgC = nc.dram_tensor("dbgC", [2, NCH, 2, 128, 512], BF16, kind="ExternalOutput").ap() \
        if debug == "c4t" else None
    dbgD = nc.dram_tensor("dbgD", [128, 2 * TOTC], F32, kind="ExternalOutput").ap() \
        if debug == "c4t" else None
    dbgE = nc.dram_tensor("dbgE", [128, 2688], F32, kind="ExternalOutput").ap() \
        if debug == "c6s" else None
    dbgE16 = nc.dram_tensor("dbgE16", [128, 9216], BF16, kind="ExternalOutput").ap() \
        if debug == "c6s" else None
    cc_in = nc.dram_tensor("cc_in", [128, NCORES * TOTC], F32, kind="Internal").ap()
    cc_out = nc.dram_tensor("cc_out", [128, NCORES * TOTC], F32, kind="Internal").ap()
    mrg_dram = nc.dram_tensor("mrg_dram", [2, NCH, 2, 128, 512], BF16, kind="Internal").ap()
    x3_dram = nc.dram_tensor("x3_dram", [NCH, 128, NK], F32, kind="Internal").ap()

    cst = E(nc.sbuf_tensor("cst", [128, NCONST], F32))
    onesr_bf = E(nc.sbuf_tensor("onesr_bf", [1, 128], BF16))
    onesr_f = E(nc.sbuf_tensor("onesr_f", [1, 128], F32))
    onesc_bf = E(nc.sbuf_tensor("onesc_bf", [128, 1], BF16))
    onest_f = E(nc.sbuf_tensor("onest_f", [128, 512], F32))
    tm_t = E(nc.sbuf_tensor("tm_t", [128, 3 * NCH], F32))
    f_col = E(nc.sbuf_tensor("f_col", [128, NCH], F32))
    eu_col = E(nc.sbuf_tensor("eu_col", [128, NCH], F32))
    scl_t = E(nc.sbuf_tensor("scl_t", [1, 2], F32))
    sclb = E(nc.sbuf_tensor("sclb", [1, 2], F32))
    qscal = E(nc.sbuf_tensor("qscal", [128, 2], F32))

    pb = [E(nc.psum_tensor(f"pb{i}", [128, 512], F32)) for i in range(8)]

    def bank(stream, i):
        b.wait(stream, bfree[i])
        return pb[i]

    block = E(nc.Block())

    b.phase = "P0"
    ld0 = None
    for t_, d_ in [(cst[:], consts), (onesr_bf[:], ones_bf_in), (onesr_f[:], ones_f_in),
                   (onesc_bf[:], ones_col_in), (onest_f[:], onesT_in),
                   (tm_t[:], tm_in), (f_col[:], td_in), (eu_col[:], tf_in),
                   (scl_t[:], scl_in)]:
        ld0 = b.dma("S", "c0", t_, d_)
    b.wait("A", ld0)
    b.op("A", lambda h: h.activation(out=f_col[:], in_=f_col[:], func=AF.Exp))
    ev_f = b.op("A", lambda h: h.activation(out=f_col[:], in_=f_col[:], func=AF.Exp,
                                            scale=cst[:, C_N1:C_N1 + 1]), ev=True)
    ev_eu = b.op("A", lambda h: h.activation(out=eu_col[:], in_=eu_col[:], func=AF.Exp), ev=True)
    b.wait("V", ld0)
    b.op("V", lambda h: h.reciprocal(out=sclb[0:1, 0:1], in_=scl_t[0:1, 0:1]))
    ev_sclb = b.op("V", lambda h: h.tensor_tensor(out=sclb[0:1, 1:2], in0=scl_t[0:1, 0:1],
                                                  in1=scl_t[0:1, 1:2], op=OP.mult), ev=True)
    b.wait("T", ev_sclb)
    ev_tq = b.op("T", lambda h: h.matmul(out=pb[7][:, 0:2], lhsT=onesr_f[0:1, 0:128],
                                         rhs=sclb[0:1, 0:2], start=True, stop=True), ev=True)
    b.wait("V", ev_tq)
    ev_qs = b.op("V", lambda h: h.tensor_copy(out=qscal[:], in_=pb[7][:, 0:2]), ev=True)
    bfree[7] = ev_qs

    b.phase = "A1"
    stWK = mkstack()
    wkvsr_all = stWK.enter_context(nc.sbuf_tensor("wkvsr_all", [128, NCH * KEPT], BF16))
    stA = mkstack()
    xk_all = stA.enter_context(nc.sbuf_tensor("xk_all", [128, NCH * SCAN], BF16))
    xv_all = stA.enter_context(nc.sbuf_tensor("xv_all", [128, NCH * SCAN], BF16))
    sr_all = stA.enter_context(nc.sbuf_tensor("sr_all", [128, NCH * KEPT], BF16))
    mbA = stA.enter_context(nc.sbuf_tensor("mbA", [1, SCAN], BF16))
    ldmb = b.dma("S", "c0", mbA[:], maskb)
    stXR = mkstack()
    xr_all = stXR.enter_context(nc.sbuf_tensor("xr_all", [128, NCH * KEPT], BF16))
    stA1 = mkstack()
    xrow = [stA1.enter_context(nc.sbuf_tensor(f"xrow{i}", [128, XR], F32)) for i in range(2)]
    tmpm = stA1.enter_context(nc.sbuf_tensor("tmpm", [128, SCAN], F32))

    mix_ev = [None] * NCH
    for e in range(NCH):
        ldx = b.dma("S", f"xr{e % 2}", xrow[e % 2][:], xT[e * 128:(e + 1) * 128, :],
                    free=mix_ev[e - 2] if e >= 2 else None)
        b.wait("V", ldx)
        xc, xl = xrow[e % 2][:, 1:XR], xrow[e % 2][:, 0:XR - 1]
        b.op("V", lambda h, xc=xc, xl=xl: h.tensor_tensor(out=tmpm[:], in0=xc, in1=xl,
                                                          op=OP.subtract))
        b.op("V", lambda h, e=e, xl=xl: h.scalar_tensor_tensor(
            out=xk_all[:, e * SCAN:(e + 1) * SCAN], in0=tmpm[:],
            scalar=tm_t[:, e:e + 1], in1=xl, op0=OP.mult, op1=OP.add))
        b.op("V", lambda h, e=e, xl=xl: h.scalar_tensor_tensor(
            out=xv_all[:, e * SCAN:(e + 1) * SCAN], in0=tmpm[:],
            scalar=tm_t[:, NCH + e:NCH + e + 1], in1=xl, op0=OP.mult, op1=OP.add))
        mix_ev[e] = b.op("V", lambda h, e=e, xl=xl: h.scalar_tensor_tensor(
            out=xr_all[:, e * KEPT:(e + 1) * KEPT], in0=tmpm[:, WARM:SCAN],
            scalar=tm_t[:, 2 * NCH + e:2 * NCH + e + 1], in1=xl[:, WARM:SCAN],
            op0=OP.mult, op1=OP.add), ev=True)
    ev_mix = mix_ev[-1]
    b.barrier()
    if debug == "xr0":
        lo = None
        for e in range(NCH):
            lo = b.dma("S", "dbgo", dbg16[:, e * SCAN:e * SCAN + KEPT],
                       xr_all[:, e * KEPT:(e + 1) * KEPT])
        b.wait("G", lo)
        b.op("G", lambda h: h.engine_nop())
        b.emit(block)
        close_all()
        es.close()
        return nc
    closestack(stA1)

    def wstream_f32(tag, stage, dram_chunk, dst_bf, state):
        for j in range(4):
            r = j % 2
            ld = b.dma("S", f"{tag}{r}", stage[:, r * 512:(r + 1) * 512],
                       dram_chunk[:, j * 512:(j + 1) * 512], free=state.get(f"c{r}"))
            b.wait("G", ld)
            if j == 0:
                b.wait("G", state.get("dst_free"))
            state[f"c{r}"] = b.op("G", lambda h, j=j, r=r: h.tensor_copy(
                out=dst_bf[:, j * 512:(j + 1) * 512],
                in_=stage[:, r * 512:(r + 1) * 512]), ev=True)
        state["conv"] = state["c1"]
        return state["c1"]

    b.phase = "A2"
    stA2 = mkstack()
    wstg = stA2.enter_context(nc.sbuf_tensor("wstg", [128, 1024], F32))
    wbfA = [stA2.enter_context(nc.sbuf_tensor(f"wbfA{i}", [128, D], BF16)) for i in range(2)]
    srt = stA2.enter_context(nc.sbuf_tensor("srt", [128, 1024], F32))

    wsR = {}
    sr_ev = {}
    last_mm = {}
    b.wait("T", ev_mix)
    for e in range(NCH):
        wsR["dst_free"] = last_mm.get(e - 2)
        conv = wstream_f32("wS", wstg, wrE[e], wbfA[e % 2], wsR)
        b.wait("T", conv)
        for ti in range(3):
            ks, kw, ko = KTILES[ti]
            g = 3 * e + ti
            pk = bank("T", g % 2)
            def mmr(h, e=e, kw=kw, ko=ko, pk=pk):
                ins = None
                for kd in range(NCH):
                    ins = h.matmul(out=pk[:, 0:kw],
                                   lhsT=wbfA[e % 2][:, kd * 128:(kd + 1) * 128],
                                   rhs=xr_all[:, kd * KEPT + ko:kd * KEPT + ko + kw],
                                   start=(kd == 0), stop=(kd == NCH - 1))
                return ins
            evm = b.op("T", mmr, ev=True)
            last_mm[e] = evm
            b.wait("A", evm)
            if g >= 2:
                b.wait("A", sr_ev[g - 2])
            eva = b.op("A", lambda h, g=g, kw=kw, pk=pk: h.activation(
                out=srt[:, (g % 2) * 512:(g % 2) * 512 + kw], in_=pk[:, 0:kw],
                func=AF.Tanh, scale=0.5), ev=True)
            bfree[g % 2] = eva
            b.wait("V", eva)
            sr_ev[g] = b.op("V", lambda h, e=e, ko=ko, kw=kw, g=g: h.tensor_scalar(
                out=sr_all[:, e * KEPT + ko:e * KEPT + ko + kw],
                in0=srt[:, (g % 2) * 512:(g % 2) * 512 + kw],
                scalar1=0.5, scalar2=0.5, op0=OP.mult, op1=OP.add), ev=True)
    closestack(stA2)
    b.barrier()

    if debug == "xr":
        lo = None
        for e in range(NCH):
            lo = b.dma("S", "dbgo", dbg16[:, e * SCAN:e * SCAN + KEPT],
                       xr_all[:, e * KEPT:(e + 1) * KEPT])
        b.wait("G", lo)
        b.op("G", lambda h: h.engine_nop())
        b.emit(block)
        close_all()
        es.close()
        return nc
    closestack(stXR)

    if debug in ("xk", "xv", "sr"):
        src_t = {"xk": xk_all, "xv": xv_all, "sr": sr_all}[debug]
        wdt = SCAN if debug in ("xk", "xv") else KEPT
        lo = None
        for e in range(NCH):
            lo = b.dma("S", "dbgo", dbg16[:, e * SCAN:e * SCAN + wdt],
                       src_t[:, e * wdt:(e + 1) * wdt])
        b.wait("G", lo)
        b.op("G", lambda h: h.engine_nop())
        b.emit(block)
        close_all()
        es.close()
        return nc

    b.phase = "A3"
    stA3 = mkstack()
    wstg3 = stA3.enter_context(nc.sbuf_tensor("wstg3", [128, 1024], F32))
    wstgK = stA3.enter_context(nc.sbuf_tensor("wstgK", [128, 1024], F32))
    wbfK = [stA3.enter_context(nc.sbuf_tensor(f"wbfK{i}", [128, D], BF16)) for i in range(2)]
    wbfV = [stA3.enter_context(nc.sbuf_tensor(f"wbfV{i}", [128, D], BF16)) for i in range(2)]
    fbc = stA3.enter_context(nc.sbuf_tensor("fbc", [128, 512], F32))
    ekt = stA3.enter_context(nc.sbuf_tensor("ekt", [128, 512], F32))
    vst = stA3.enter_context(nc.sbuf_tensor("vst", [128, 512], F32))
    gvt = stA3.enter_context(nc.sbuf_tensor("gvt", [128, 512], F32))
    csA = stA3.enter_context(nc.sbuf_tensor("csA", [128, 513], F32))
    csB = stA3.enter_context(nc.sbuf_tensor("csB", [128, 513], F32))
    stAB = stA3.enter_context(nc.sbuf_tensor("stAB", [128, 2], F32))
    numt = stA3.enter_context(nc.sbuf_tensor("numt", [128, 512], F32))
    dent = stA3.enter_context(nc.sbuf_tensor("dent", [128, 512], F32))

    dbgA = None
    if debug == "a3i":
        dbgA = stA3.enter_context(nc.sbuf_tensor("dbgA", [128, 3074], F32))
    wsK = {"c0": wsR.get("c0"), "c1": wsR.get("c1")}
    wsV = {}
    last_wkv = {}
    ekt_rd = None
    fbc_rd = None
    for e in range(NCH):
        wsK["dst_free"] = last_wkv.get(e - 2)
        convK = wstream_f32("wS", wstgK, wkE[e], wbfK[e % 2], wsK)
        wsV["dst_free"] = last_wkv.get(e - 2)
        convV = wstream_f32("wS2", wstg3, wvE[e], wbfV[e % 2], wsV)
        b.wait("V", fbc_rd)
        if e == 0:
            b.wait("V", ev_f)
            b.wait("V", ev_eu)
        b.op("V", lambda h, e=e: h.tensor_scalar(
            out=fbc[:], in0=onest_f[:], scalar1=f_col[:, e:e + 1], scalar2=None, op0=OP.mult))
        b.op("V", lambda h: h.tensor_scalar(out=stAB[:, 0:1], in0=onest_f[:, 0:1],
                                            scalar1=0.0, scalar2=None, op0=OP.mult))
        b.op("V", lambda h: h.tensor_scalar(out=stAB[:, 1:2], in0=onest_f[:, 0:1],
                                            scalar1=0.0, scalar2=None, op0=OP.mult))
        b.wait("T", convK)
        b.wait("T", convV)
        if e == 0:
            b.wait("T", ldmb)
        for ti in range(3):
            ts_, tw = TILES[ti]
            g = 3 * e + ti
            pk = bank("T", g % 2)
            def mmk(h, e=e, ts_=ts_, tw=tw, pk=pk):
                for kd in range(NCH):
                    h.matmul(out=pk[:, 0:tw], lhsT=wbfK[e % 2][:, kd * 128:(kd + 1) * 128],
                             rhs=xk_all[:, kd * SCAN + ts_:kd * SCAN + ts_ + tw],
                             start=(kd == 0), stop=False)
                return h.matmul(out=pk[:, 0:tw], lhsT=onesr_bf[0:1, 0:128],
                                rhs=mbA[0:1, ts_:ts_ + tw], start=False, stop=True)
            ev_k = b.op("T", mmk, ev=True)
            pv = bank("T", 2 + g % 2)
            def mmv(h, e=e, ts_=ts_, tw=tw, pv=pv):
                ins = None
                for kd in range(NCH):
                    ins = h.matmul(out=pv[:, 0:tw], lhsT=wbfV[e % 2][:, kd * 128:(kd + 1) * 128],
                                   rhs=xv_all[:, kd * SCAN + ts_:kd * SCAN + ts_ + tw],
                                   start=(kd == 0), stop=(kd == NCH - 1))
                return ins
            ev_v = b.op("T", mmv, ev=True)
            b.wait("A", ev_k)
            b.wait("A", ekt_rd)
            ev_ek = b.op("A", lambda h, tw=tw, pk=pk: h.activation(
                out=ekt[:, 0:tw], in_=pk[:, 0:tw], func=AF.Exp), ev=True)
            bfree[g % 2] = ev_ek
            b.wait("V", ev_v)
            ev_vs = b.op("V", lambda h, tw=tw, pv=pv: h.tensor_copy(
                out=vst[:, 0:tw], in_=pv[:, 0:tw]), ev=True)
            bfree[2 + g % 2] = ev_vs
            b.wait("V", ev_ek)
            b.op("V", lambda h, tw=tw: h.tensor_tensor(
                out=gvt[:, 0:tw], in0=ekt[:, 0:tw], in1=vst[:, 0:tw], op=OP.mult))
            b.op("V", lambda h, ti=ti: h.tensor_copy(
                out=csA[:, 0:1], in_=stAB[:, 0:1]))
            b.op("V", lambda h, ti=ti: h.tensor_copy(
                out=csB[:, 0:1], in_=stAB[:, 1:2]))
            b.op("V", lambda h, tw=tw: h.tensor_tensor_scan(
                out=csA[:, 1:1 + tw], data0=fbc[:, 0:tw], data1=gvt[:, 0:tw],
                initial=csA[:, 0:1], op0=OP.mult, op1=OP.add))
            b.op("V", lambda h, tw=tw: h.tensor_tensor_scan(
                out=csB[:, 1:1 + tw], data0=fbc[:, 0:tw], data1=ekt[:, 0:tw],
                initial=csB[:, 0:1], op0=OP.mult, op1=OP.add))
            ks, kw, ko = KTILES[ti]
            loc = ks - ts_
            ekt_rd = b.op("V", lambda h, loc=loc, kw=kw, e=e: h.tensor_scalar(
                out=gvt[:, 0:kw], in0=ekt[:, loc:loc + kw],
                scalar1=eu_col[:, e:e + 1], scalar2=None, op0=OP.mult), ev=True)
            b.op("V", lambda h, tw=tw: h.tensor_copy(out=stAB[:, 0:1], in_=csA[:, tw:tw + 1]))
            fbc_rd = b.op("V", lambda h, tw=tw: h.tensor_copy(out=stAB[:, 1:2],
                                                              in_=csB[:, tw:tw + 1]), ev=True)
            ev_den = b.op("V", lambda h, loc=loc, kw=kw: h.tensor_tensor(
                out=dent[:, 0:kw], in0=gvt[:, 0:kw], in1=csB[:, loc:loc + kw], op=OP.add),
                ev=True)
            b.wait("A", ev_den)
            b.op("A", lambda h, kw=kw: h.activation(
                out=dent[:, 0:kw], in_=dent[:, 0:kw], func=AF.Ln,
                bias=cst[:, C_EPS30:C_EPS30 + 1]))
            ev_rc = b.op("A", lambda h, kw=kw: h.activation(
                out=dent[:, 0:kw], in_=dent[:, 0:kw], func=AF.Exp,
                scale=cst[:, C_N1:C_N1 + 1]), ev=True)
            b.op("V", lambda h, loc=loc, kw=kw: h.tensor_tensor(
                out=numt[:, 0:kw], in0=gvt[:, 0:kw], in1=vst[:, loc:loc + kw], op=OP.mult))
            b.op("V", lambda h, loc=loc, kw=kw: h.tensor_tensor(
                out=numt[:, 0:kw], in0=numt[:, 0:kw], in1=csA[:, loc:loc + kw], op=OP.add))
            b.wait("V", ev_rc)
            b.op("V", lambda h, kw=kw: h.tensor_tensor(
                out=numt[:, 0:kw], in0=numt[:, 0:kw], in1=dent[:, 0:kw], op=OP.mult))
            last_wkv[e] = b.op("V", lambda h, e=e, ko=ko, kw=kw: h.tensor_tensor(
                out=wkvsr_all[:, e * KEPT + ko:e * KEPT + ko + kw],
                in0=numt[:, 0:kw], in1=sr_all[:, e * KEPT + ko:e * KEPT + ko + kw],
                op=OP.mult), ev=True)
    if debug == "a3i":
        b.barrier()
        lo = b.dma("S", "dbgo", dbgF[:], dbgA[:])
        b.wait("G", lo)
        b.op("G", lambda h: h.engine_nop())
        b.emit(block)
        close_all()
        es.close()
        return nc
    b.barrier()
    closestack(stA3)
    closestack(stA)

    def emit_ln2(y_view, kept_w, ktiles, ln_bufs):
        lnrow, bct, lnybt, lnysq = ln_bufs
        slot_rd = {}
        for stat in range(2):
            for ti, (ks_, kw_, ko_) in enumerate(ktiles):
                pk = bank("T", 6 + stat)
                evm = None
                for m in range(NCH):
                    sl = (m % 2) * 512
                    src = y_view(m)[:, ko_:ko_ + kw_]
                    weng = "V" if stat == 0 else "A"
                    b.wait(weng, slot_rd.get((stat, sl)))
                    if stat == 0:
                        evc = b.op("V", lambda h, src=src, sl=sl, kw_=kw_: h.tensor_copy(
                            out=lnybt[:, sl:sl + kw_], in_=src), ev=True)
                        dt_ = lnybt
                    else:
                        evc = b.op("A", lambda h, src=src, sl=sl, kw_=kw_: h.activation(
                            out=lnysq[:, sl:sl + kw_], in_=src, func=AF.Square), ev=True)
                        dt_ = lnysq
                    b.wait("T", evc)
                    evm = b.op("T", lambda h, sl=sl, kw_=kw_, m=m, pk=pk, dt_=dt_: h.matmul(
                        out=pk[0:1, 0:kw_], lhsT=onesc_bf[:, 0:1], rhs=dt_[:, sl:sl + kw_],
                        start=(m == 0), stop=(m == NCH - 1)), ev=True)
                    slot_rd[(stat, sl)] = evm
                b.wait("V", evm)
                eve = b.op("V", lambda h, stat=stat, ko_=ko_, kw_=kw_, pk=pk: h.tensor_scalar(
                    out=lnrow[0:1, stat * KEPT + ko_:stat * KEPT + ko_ + kw_],
                    in0=pk[0:1, 0:kw_], scalar1=1.0 / D, scalar2=None, op0=OP.mult), ev=True)
                bfree[6 + stat] = eve
        W = kept_w
        b.op("V", lambda h, W=W: h.tensor_tensor(
            out=lnrow[0:1, 2 * KEPT:2 * KEPT + W], in0=lnrow[0:1, 0:W],
            in1=lnrow[0:1, 0:W], op=OP.mult))
        evv = b.op("V", lambda h, W=W: h.tensor_tensor(
            out=lnrow[0:1, KEPT:KEPT + W], in0=lnrow[0:1, KEPT:KEPT + W],
            in1=lnrow[0:1, 2 * KEPT:2 * KEPT + W], op=OP.subtract), ev=True)
        b.wait("A", evv)
        b.op("A", lambda h, W=W: h.activation(out=lnrow[0:1, KEPT:KEPT + W],
                                              in_=lnrow[0:1, KEPT:KEPT + W], func=AF.Ln,
                                              bias=cst[0:1, C_EPS5:C_EPS5 + 1]))
        evi = b.op("A", lambda h, W=W: h.activation(out=lnrow[0:1, KEPT:KEPT + W],
                                                    in_=lnrow[0:1, KEPT:KEPT + W],
                                                    func=AF.Exp,
                                                    scale=cst[0:1, C_NHALF:C_NHALF + 1]),
                   ev=True)
        b.wait("V", evi)
        evb = b.op("V", lambda h, W=W: h.scalar_tensor_tensor(
            out=lnrow[0:1, 0:W], in0=lnrow[0:1, 0:W], scalar=-1.0,
            in1=lnrow[0:1, KEPT:KEPT + W], op0=OP.mult, op1=OP.mult), ev=True)
        b.wait("T", evi)
        b.wait("T", evb)
        ev_apply = None
        for ti, (ks_, kw_, ko_) in enumerate(ktiles):
            p0 = bank("T", 4)
            ev0 = b.op("T", lambda h, ko_=ko_, kw_=kw_, p0=p0: h.matmul(
                out=p0[:, 0:kw_], lhsT=onesr_f[0:1, 0:128],
                rhs=lnrow[0:1, KEPT + ko_:KEPT + ko_ + kw_], start=True, stop=True), ev=True)
            p1 = bank("T", 5)
            ev1 = b.op("T", lambda h, ko_=ko_, kw_=kw_, p1=p1: h.matmul(
                out=p1[:, 0:kw_], lhsT=onesr_f[0:1, 0:128],
                rhs=lnrow[0:1, ko_:ko_ + kw_], start=True, stop=True), ev=True)
            b.wait("V", ev0)
            b.op("V", lambda h, kw_=kw_, p0=p0: h.tensor_copy(out=bct[:, 0:kw_],
                                                              in_=p0[:, 0:kw_]))
            b.wait("V", ev1)
            evbc = b.op("V", lambda h, kw_=kw_, p1=p1: h.tensor_copy(
                out=bct[:, 512:512 + kw_], in_=p1[:, 0:kw_]), ev=True)
            bfree[4] = evbc
            bfree[5] = evbc
            for m in range(NCH):
                y = y_view(m)
                b.op("V", lambda h, y=y, ko_=ko_, kw_=kw_: h.tensor_tensor(
                    out=y[:, ko_:ko_ + kw_], in0=y[:, ko_:ko_ + kw_],
                    in1=bct[:, 0:kw_], op=OP.mult))
                ev_apply = b.op("V", lambda h, y=y, ko_=ko_, kw_=kw_: h.tensor_tensor(
                    out=y[:, ko_:ko_ + kw_], in0=y[:, ko_:ko_ + kw_],
                    in1=bct[:, 512:512 + kw_], op=OP.add), ev=True)
        return ev_apply

    b.phase = "A4"
    stY = mkstack()
    Y1 = stY.enter_context(nc.sbuf_tensor("Y1", [128, NCH * KEPT], F32, side="right"))
    stA4 = mkstack()
    wstg4 = stA4.enter_context(nc.sbuf_tensor("wstg4", [128, 1024], F32))
    wbfO = [stA4.enter_context(nc.sbuf_tensor(f"wbfO{i}", [128, D], BF16)) for i in range(2)]
    xrd = [stA4.enter_context(nc.sbuf_tensor(f"xrd{i}", [128, KEPT], F32)) for i in range(2)]

    wsO = {}
    y_ev = {}
    for m in range(NCH):
        wsO["dst_free"] = y_ev.get(m - 2)
        convO = wstream_f32("wS", wstg4, woE[m], wbfO[m % 2], wsO)
        ldx = b.dma("S", f"xd{m % 2}", xrd[m % 2][:], xT[m * 128:(m + 1) * 128, WARM + 1:XR],
                    free=y_ev.get(m - 2))
        b.wait("T", convO)
        for ti, (ks, kw, ko) in enumerate(KTILES):
            g = 3 * m + ti
            pk = bank("T", g % 2)
            def mmo(h, m=m, kw=kw, ko=ko, pk=pk):
                ins = None
                for kd in range(NCH):
                    ins = h.matmul(out=pk[:, 0:kw],
                                   lhsT=wbfO[m % 2][:, kd * 128:(kd + 1) * 128],
                                   rhs=wkvsr_all[:, kd * KEPT + ko:kd * KEPT + ko + kw],
                                   start=(kd == 0), stop=(kd == NCH - 1))
                return ins
            evm = b.op("T", mmo, ev=True)
            b.wait("V", evm)
            b.wait("V", ldx)
            y_ev[m] = b.op("V", lambda h, m=m, ko=ko, kw=kw, pk=pk: h.tensor_tensor(
                out=Y1[:, m * KEPT + ko:m * KEPT + ko + kw],
                in0=xrd[m % 2][:, ko:ko + kw], in1=pk[:, 0:kw], op=OP.add), ev=True)
            bfree[g % 2] = y_ev[m]
    if debug in ("wkvsr", "y"):
        b.barrier()
        lo = None
        for e in range(NCH):
            if debug == "wkvsr":
                lo = b.dma("S", "dbgo", dbg16[:, e * SCAN:e * SCAN + KEPT],
                           wkvsr_all[:, e * KEPT:(e + 1) * KEPT])
            else:
                lo = b.dma("S", "dbgo", dbg[e * 128:(e + 1) * 128, :],
                           Y1[:, e * KEPT:(e + 1) * KEPT])
        b.wait("G", lo)
        b.op("G", lambda h: h.engine_nop())
        b.emit(block)
        close_all()
        es.close()
        return nc

    b.barrier()
    closestack(stA4)
    closestack(stWK)

    stLN = mkstack()
    lnrow = stLN.enter_context(nc.sbuf_tensor("lnrow", [1, 3 * KEPT], F32))
    bct = stLN.enter_context(nc.sbuf_tensor("bct", [128, 1024], F32))
    lnybt = stLN.enter_context(nc.sbuf_tensor("lnybt", [128, 1024], BF16))
    lnysq = stLN.enter_context(nc.sbuf_tensor("lnysq", [128, 1024], BF16))
    LNB = (lnrow, bct, lnybt, lnysq)

    y1v = lambda m: Y1[:, m * KEPT:(m + 1) * KEPT]
    b.phase = "LN1"
    ev_x1 = emit_ln2(y1v, KEPT, KTILES, LNB)
    b.phase = "B"

    def dump_debug(ev_done, with_halo=True):
        b.wait("S", ev_done)
        lo = lo2 = None
        for m in range(NCH):
            if with_halo:
                lo = b.dma("S", "dbgo", dbg[m * 128:(m + 1) * 128, :], y1v(m))
            lo2 = b.dma("S", "outo", out[m * 128:(m + 1) * 128, :],
                        y1v(m)[:, HALO:KEPT] if with_halo else y1v(m))
        if lo is not None:
            b.wait("G", lo)
        b.wait("G", lo2)
        b.op("G", lambda h: h.engine_nop())
        b.emit(block)
        close_all()
        es.close()
        return nc

    if debug == "x1":
        return dump_debug(ev_x1)

    b.phase = "B"
    b.barrier()
    stB = mkstack()
    x1qb = stB.enter_context(nc.sbuf_tensor("x1qb", [128, NCH * KEPT], BF16))
    qtmp = stB.enter_context(nc.sbuf_tensor("qtmp", [128, KEPT], F32))
    q8 = stB.enter_context(nc.sbuf_tensor("q8", [128, KEPT], F8))
    w8st = stB.enter_context(nc.sbuf_tensor("w8st", [128, D], F8))
    wbfF = [stB.enter_context(nc.sbuf_tensor(f"wbfF{i}", [128, D], BF16)) for i in range(2)]
    rtmp = stB.enter_context(nc.sbuf_tensor("rtmp", [128, 512], F32))

    b.wait("V", ev_qs)
    b.wait("V", ev_x1)
    evq_last = None
    ev_q8rd = None
    for e in range(NCH):
        b.wait("V", ev_q8rd)
        b.op("V", lambda h, e=e: h.tensor_scalar(
            out=qtmp[:], in0=Y1[:, e * KEPT:(e + 1) * KEPT],
            scalar1=qscal[:, 0:1], scalar2=448.0, op0=OP.mult, op1=OP.min))
        evq1 = b.op("V", lambda h: h.tensor_scalar(
            out=qtmp[:], in0=qtmp[:], scalar1=-448.0, scalar2=None, op0=OP.max), ev=True)
        b.wait("A", evq1)
        ev_q8rd = b.op("A", lambda h: h.copy(out=q8[:], in_=qtmp[:]), ev=True)
        evq_last = b.op("A", lambda h, e=e: h.copy(
            out=x1qb[:, e * KEPT:(e + 1) * KEPT], in_=q8[:]), ev=True)
    ev_qall = evq_last

    convF = None
    y2_ev = {}
    for m in range(NCH):
        ldf = b.dma("S", "w8", w8st[:], f8E[m], free=convF)
        b.wait("G", ldf)
        b.wait("G", y2_ev.get(m - 2))
        convF = b.op("G", lambda h, m=m: h.tensor_copy(out=wbfF[m % 2][:], in_=w8st[:]), ev=True)
        b.wait("T", convF)
        if m == 0:
            b.wait("T", ev_qall)
        for ti, (ks, kw, ko) in enumerate(KTILES):
            g = 3 * m + ti
            pk = bank("T", g % 2)
            def mmf(h, m=m, kw=kw, ko=ko, pk=pk):
                ins = None
                for kd in range(NCH):
                    ins = h.matmul(out=pk[:, 0:kw],
                                   lhsT=wbfF[m % 2][:, kd * 128:(kd + 1) * 128],
                                   rhs=x1qb[:, kd * KEPT + ko:kd * KEPT + ko + kw],
                                   start=(kd == 0), stop=(kd == NCH - 1))
                return ins
            evm = b.op("T", mmf, ev=True)
            b.wait("V", evm)
            b.op("V", lambda h, kw=kw, pk=pk: h.tensor_scalar(
                out=rtmp[:, 0:kw], in0=pk[:, 0:kw], scalar1=qscal[:, 1:2], scalar2=None,
                op0=OP.mult))
            y2_ev[m] = b.op("V", lambda h, m=m, ko=ko, kw=kw: h.tensor_tensor(
                out=Y1[:, m * KEPT + ko:m * KEPT + ko + kw],
                in0=Y1[:, m * KEPT + ko:m * KEPT + ko + kw],
                in1=rtmp[:, 0:kw], op=OP.add), ev=True)
            bfree[g % 2] = y2_ev[m]
    b.barrier()
    closestack(stB)
    b.phase = "LN2"
    ev_x2 = emit_ln2(y1v, KEPT, KTILES, LNB)

    if debug == "x2":
        return dump_debug(ev_x2)

    b.phase = "C1"
    b.barrier()
    stC = mkstack()
    q_all = stC.enter_context(nc.sbuf_tensor("q_all", [128, 4 * NK], BF16))
    zc_all = stC.enter_context(nc.sbuf_tensor("zc_all", [128, 3 * NK], F32))
    pw_t = stC.enter_context(nc.sbuf_tensor("pw_t", [32, NK], F32))
    tot_t = stC.enter_context(nc.sbuf_tensor("tot_t", [128, TOTC], F32))
    tot0_t = stC.enter_context(nc.sbuf_tensor("tot0_t", [128, 2 * NCH], F32))
    off_t = stC.enter_context(nc.sbuf_tensor("off_t", [128, TOTC], F32))
    feat_t = stC.enter_context(nc.sbuf_tensor("feat_t", [32, NK], F32))
    slop_t = stC.enter_context(nc.sbuf_tensor("slop_t", [32, 2], F32))
    ssc_t = stC.enter_context(nc.sbuf_tensor("ssc_t", [32, 1], F32))
    ck_t = stC.enter_context(nc.sbuf_tensor("ck_t", [128, 12], F32))
    theta_t = stC.enter_context(nc.sbuf_tensor("theta_t", [128, NCH], F32))
    mcon_t = stC.enter_context(nc.sbuf_tensor("mcon_t", [128, NCORES], F32))
    msel_t = stC.enter_context(nc.sbuf_tensor("msel_t", [128, NCORES], F32))
    stC1 = mkstack()
    z_all = stC1.enter_context(nc.sbuf_tensor("z_all", [128, 3 * KEPT], F32))
    x2b = stC1.enter_context(nc.sbuf_tensor("x2b", [128, NCH * KEPT], BF16))
    wmem_t = stC1.enter_context(nc.sbuf_tensor("wmem_t", [128, NCH * 384], BF16))
    wqp_t = stC1.enter_context(nc.sbuf_tensor("wqp_t", [128, NCH * 512], BF16))
    wstgC = stC1.enter_context(nc.sbuf_tensor("wstgC", [128, 1024], F32))

    ldcs = None
    for t_, d_ in [(feat_t[:], feat_in), (slop_t[0:32, 0:1], sslop_in), (ssc_t[:], ssc_in),
                   (ck_t[:], ck_in), (theta_t[:], theta_in), (mcon_t[:], mcon_in),
                   (msel_t[:], msel_in)]:
        ldcs = b.dma("S", "ccs", t_, d_)

    x2b_ev = None
    b.wait("A", ev_x2)
    for e in range(NCH):
        x2b_ev = b.op("A", lambda h, e=e: h.copy(
            out=x2b[:, e * KEPT:(e + 1) * KEPT], in_=Y1[:, e * KEPT:(e + 1) * KEPT]), ev=True)
    wsM = {}
    pieces = []
    for c_ in range(12):
        pieces.append((wmem_in[:, c_ * 512:(c_ + 1) * 512],
                       wmem_t[:, c_ * 512:(c_ + 1) * 512]))
    for c_ in range(16):
        pieces.append((wqp_in[:, c_ * 512:(c_ + 1) * 512],
                       wqp_t[:, c_ * 512:(c_ + 1) * 512]))
    for j, (src_, dst_) in enumerate(pieces):
        r = j % 2
        ld = b.dma("S", f"wS{r}", wstgC[:, r * 512:(r + 1) * 512], src_,
                   free=wsM.get(f"c{r}"))
        b.wait("G", ld)
        wsM[f"c{r}"] = b.op("G", lambda h, src_=src_, dst_=dst_, r=r: h.tensor_copy(
            out=dst_, in_=wstgC[:, r * 512:(r + 1) * 512]), ev=True)
    wsM["conv"] = wsM["c1"]
    ev_wmq = wsM["c1"]

    ZT = [(0, 512), (512, 512), (1024, 3)]
    b.wait("T", x2b_ev)
    b.wait("T", ev_wmq)
    ev_z = None
    zb = 0
    for (zs, zw) in ZT:
        for zc in range(3):
            pk = bank("T", zb % 2)
            def mmz(h, zs=zs, zw=zw, zc=zc, pk=pk):
                ins = None
                for kd in range(NCH):
                    ins = h.matmul(out=pk[:, 0:zw],
                                   lhsT=wmem_t[:, kd * 384 + zc * 128:kd * 384 + (zc + 1) * 128],
                                   rhs=x2b[:, kd * KEPT + zs:kd * KEPT + zs + zw],
                                   start=(kd == 0), stop=(kd == NCH - 1))
                return ins
            evm = b.op("T", mmz, ev=True)
            b.wait("V", evm)
            ev_z = b.op("V", lambda h, zs=zs, zw=zw, zc=zc, pk=pk: h.tensor_copy(
                out=z_all[:, zc * KEPT + zs:zc * KEPT + zs + zw], in_=pk[:, 0:zw]), ev=True)
            bfree[zb % 2] = ev_z
            zb += 1
    ev_qp = None
    for (ls, lw) in LKT:
        for qc in range(4):
            pk = bank("T", zb % 2)
            def mmq(h, ls=ls, lw=lw, qc=qc, pk=pk):
                ins = None
                for kd in range(NCH):
                    ins = h.matmul(out=pk[:, 0:lw],
                                   lhsT=wqp_t[:, kd * 512 + qc * 128:kd * 512 + (qc + 1) * 128],
                                   rhs=x2b[:, kd * KEPT + HALO + ls:kd * KEPT + HALO + ls + lw],
                                   start=(kd == 0), stop=(kd == NCH - 1))
                return ins
            evm = b.op("T", mmq, ev=True)
            b.wait("V", evm)
            ev_qp = b.op("V", lambda h, ls=ls, lw=lw, qc=qc, pk=pk: h.tensor_copy(
                out=q_all[:, qc * NK + ls:qc * NK + ls + lw], in_=pk[:, 0:lw]), ev=True)
            bfree[zb % 2] = ev_qp
            zb += 1

    b.wait("V", ldcs)
    ev_cv = None
    for zc in range(3):
        b.op("V", lambda h, zc=zc: h.tensor_scalar(
            out=zc_all[:, zc * NK:(zc + 1) * NK], in0=z_all[:, zc * KEPT:zc * KEPT + NK],
            scalar1=ck_t[:, 0 + zc:1 + zc], scalar2=None, op0=OP.mult))
        for j in range(1, 4):
            ev_cv = b.op("V", lambda h, zc=zc, j=j: h.scalar_tensor_tensor(
                out=zc_all[:, zc * NK:(zc + 1) * NK],
                in0=z_all[:, zc * KEPT + j:zc * KEPT + j + NK],
                scalar=ck_t[:, j * 3 + zc:j * 3 + zc + 1],
                in1=zc_all[:, zc * NK:(zc + 1) * NK], op0=OP.mult, op1=OP.add),
                ev=(zc == 2 and j == 3))
    b.wait("A", ev_cv)
    b.op("A", lambda h: h.activation(out=slop_t[0:32, 1:2], in_=slop_t[0:32, 0:1],
                                     func=AF.Exp))
    evsp = b.op("A", lambda h: h.activation(out=slop_t[0:32, 1:2], in_=slop_t[0:32, 1:2],
                                            func=AF.Ln, bias=cst[0:32, C_ONE:C_ONE + 1]),
                ev=True)
    b.wait("V", evsp)
    b.op("V", lambda h: h.tensor_scalar(out=slop_t[0:32, 1:2], in0=slop_t[0:32, 1:2],
                                        scalar1=-1.0, scalar2=None, op0=OP.mult))
    b.op("V", lambda h: h.tensor_scalar(out=pw_t[:], in0=zc_all[0:32, 2 * NK:3 * NK],
                                        scalar1=ssc_t[0:32, 0:1], scalar2=20.0,
                                        op0=OP.mult, op1=OP.min))
    b.op("V", lambda h: h.tensor_scalar(out=pw_t[:], in0=pw_t[:], scalar1=-20.0,
                                        scalar2=None, op0=OP.max))
    evpw = b.op("V", lambda h: h.scalar_tensor_tensor(
        out=pw_t[:], in0=feat_t[:], scalar=slop_t[0:32, 1:2], in1=pw_t[:],
        op0=OP.mult, op1=OP.add), ev=True)
    b.wait("A", evpw)
    evpw2 = b.op("A", lambda h: h.activation(out=pw_t[:], in_=pw_t[:], func=AF.Exp), ev=True)

    b.barrier()
    closestack(stC1)

    stC4 = mkstack()
    repkh_t = stC4.enter_context(nc.sbuf_tensor("repkh_t", [128, 2 * FLAT], F32))
    repk_t = stC4.enter_context(nc.sbuf_tensor("repk_t", [32, FLAT], F32))
    for t_, d_ in [(repkh_t[:, 0:FLAT], repkh_in[0]), (repkh_t[:, FLAT:2 * FLAT], repkh_in[1]),
                   (repk_t[:], repk_in)]:
        ldcs = b.dma("S", "ccs", t_, d_)
    ldcs2 = ldcs

    b.phase = "C4"
    stC4b = mkstack()
    phit = stC4b.enter_context(nc.sbuf_tensor("phit", [128, 512], F32))
    vit = stC4b.enter_context(nc.sbuf_tensor("vit", [128, 512], I32))
    rrt = stC4b.enter_context(nc.sbuf_tensor("rrt", [128, 512], F32))
    sint = stC4b.enter_context(nc.sbuf_tensor("sint", [128, 512], F32))
    cost = stC4b.enter_context(nc.sbuf_tensor("cost", [128, 512], F32))
    mgt = [stC4b.enter_context(nc.sbuf_tensor(f"mgt{i}", [128, 512], BF16)) for i in range(2)]
    acc_t = stC4b.enter_context(nc.sbuf_tensor("acc_t", [128, 2], F32))

    ev_sin_prev = None
    ev_cos_prev = None
    prev_spill = None
    b.wait("T", evpw2)
    b.wait("T", ldcs2)
    for ci in range(NCH):
        kchunk = ci // 8
        for s in range(2):
            ls, lw = LKT[s]
            pk = bank("T", 4 + (2 * ci + s) % 2)
            if ci == 0 and s == 0:
                b.wait("T", ("V", b.cnt["V"]))
            ev_kr = b.op("T", lambda h, kchunk=kchunk, ci=ci, ls=ls, lw=lw, pk=pk: h.matmul(
                out=pk[:, 0:lw],
                lhsT=repkh_t[:, kchunk * FLAT + ci * 128:kchunk * FLAT + (ci + 1) * 128],
                rhs=zc_all[:, kchunk * NK + ls:kchunk * NK + ls + lw],
                start=True, stop=True), ev=True)
            b.wait("V", ev_kr)
            b.op("V", lambda h, lw=lw, pk=pk, ci=ci: h.scalar_tensor_tensor(
                out=phit[:, 0:lw], in0=pk[:, 0:lw], scalar=theta_t[:, ci:ci + 1],
                in1=onest_f[:, 0:lw], op0=OP.mult, op1=OP.mult))
            b.op("V", lambda h, lw=lw: h.tensor_scalar(
                out=rrt[:, 0:lw], in0=phit[:, 0:lw], scalar1=1.0 / (2 * PI), scalar2=None,
                op0=OP.mult))
            b.op("V", lambda h, lw=lw: h.tensor_copy(out=vit[:, 0:lw], in_=rrt[:, 0:lw]))
            b.wait("V", ev_sin_prev)
            b.op("V", lambda h, lw=lw: h.scalar_tensor_tensor(
                out=sint[:, 0:lw], in0=vit[:, 0:lw], scalar=-(2 * PI), in1=phit[:, 0:lw],
                op0=OP.mult, op1=OP.add))
            evs1 = b.op("V", lambda h, lw=lw: h.tensor_scalar(
                out=sint[:, 0:lw], in0=sint[:, 0:lw], scalar1=PI - 1e-6,
                scalar2=-(PI - 1e-6), op0=OP.min, op1=OP.max), ev=True)
            b.op("V", lambda h, lw=lw: h.tensor_scalar(
                out=rrt[:, 0:lw], in0=rrt[:, 0:lw], scalar1=0.25, scalar2=None, op0=OP.add))
            b.op("V", lambda h, lw=lw: h.tensor_copy(out=vit[:, 0:lw], in_=rrt[:, 0:lw]))
            b.wait("V", ev_cos_prev)
            evcz = b.op("V", lambda h, lw=lw, pk=pk: h.scalar_tensor_tensor(
                out=cost[:, 0:lw], in0=vit[:, 0:lw], scalar=-(2 * PI), in1=phit[:, 0:lw],
                op0=OP.mult, op1=OP.add), ev=True)
            bfree[4 + (2 * ci + s) % 2] = evcz
            evs2 = b.op("V", lambda h, lw=lw: h.tensor_scalar(
                out=cost[:, 0:lw], in0=cost[:, 0:lw], scalar1=PI / 2 - 1e-6,
                scalar2=-(3 * PI / 2 - 1e-6), op0=OP.min, op1=OP.max), ev=True)
            b.wait("A", evs1)
            b.wait("A", evs2)
            ev_sin_prev = b.op("A", lambda h, lw=lw: h.activation(
                out=sint[:, 0:lw], in_=sint[:, 0:lw], func=AF.Sin), ev=True)
            ev_cos_prev = b.op("A", lambda h, lw=lw: h.activation(
                out=cost[:, 0:lw], in_=cost[:, 0:lw], func=AF.Sin,
                bias=cst[:, C_PIHALF:C_PIHALF + 1]), ev=True)
            pk2 = bank("T", 6 + (2 * ci + s) % 2)
            ev_pr = b.op("T", lambda h, ci=ci, ls=ls, lw=lw, pk2=pk2: h.matmul(
                out=pk2[:, 0:lw], lhsT=repk_t[0:32, ci * 128:(ci + 1) * 128],
                rhs=pw_t[0:32, ls:ls + lw], start=True, stop=True), ev=True)
            b.wait("V", ev_cos_prev)
            b.wait("V", ev_pr)
            if prev_spill is not None:
                b.wait("V", prev_spill[0])
                b.wait("V", prev_spill[1])
            b.op("V", lambda h, lw=lw, pk2=pk2: h.scalar_tensor_tensor(
                out=mgt[0][:, 0:lw], in0=cost[:, 0:lw], scalar=1.0, in1=pk2[:, 0:lw],
                op0=OP.mult, op1=OP.mult, accum_out=acc_t[:, 0:1]))
            evmi = b.op("V", lambda h, lw=lw, pk2=pk2: h.scalar_tensor_tensor(
                out=mgt[1][:, 0:lw], in0=sint[:, 0:lw], scalar=1.0, in1=pk2[:, 0:lw],
                op0=OP.mult, op1=OP.mult, accum_out=acc_t[:, 1:2]), ev=True)
            bfree[6 + (2 * ci + s) % 2] = evmi
            if s == 0:
                b.op("V", lambda h, ci=ci: h.tensor_copy(out=tot0_t[:, ci:ci + 1],
                                                         in_=acc_t[:, 0:1]))
                b.op("V", lambda h, ci=ci: h.tensor_copy(out=tot0_t[:, NCH + ci:NCH + ci + 1],
                                                         in_=acc_t[:, 1:2]))
            else:
                b.op("V", lambda h, ci=ci: h.tensor_tensor(
                    out=tot_t[:, ci:ci + 1], in0=tot0_t[:, ci:ci + 1], in1=acc_t[:, 0:1],
                    op=OP.add))
                b.op("V", lambda h, ci=ci: h.tensor_tensor(
                    out=tot_t[:, NCH + ci:NCH + ci + 1], in0=tot0_t[:, NCH + ci:NCH + ci + 1],
                    in1=acc_t[:, 1:2], op=OP.add))
            m1 = b.dma("S", "mg0", mrg_dram[0, ci, s], mgt[0][:, 0:lw], free=evmi)
            m2 = b.dma("S", "mg1", mrg_dram[1, ci, s], mgt[1][:, 0:lw], free=None)
            prev_spill = (m1, m2)
    ev_pwt = b.op("V", lambda h: h.tensor_reduce(out=tot_t[0:32, 32:33], in_=pw_t[:],
                                                 axis=AX, op=OP.add), ev=True)
    b.op("V", lambda h: h.tensor_scalar(out=tot_t[0:32, 33:34], in0=tot_t[0:32, 32:33],
                                        scalar1=0.0, scalar2=None, op0=OP.mult))
    for pbase in (32, 64, 96):
        evtt = b.op("V", lambda h, pbase=pbase: h.tensor_scalar(
            out=tot_t[pbase:pbase + 32, 32:34], in0=tot_t[pbase:pbase + 32, 30:32],
            scalar1=0.0, scalar2=None, op0=OP.mult), ev=True)

    b.phase = "CC"
    closestack(stC4b)
    stCC = mkstack()
    gg = stCC.enter_context(nc.sbuf_tensor("gg", [128, NCORES * TOTC], F32))
    evgg = None
    for j in range(NCORES):
        evgg = b.op("V", lambda h, j=j: h.tensor_scalar(
            out=gg[:, j * TOTC:(j + 1) * TOTC], in0=tot_t[:],
            scalar1=mcon_t[:, j:j + 1], scalar2=None, op0=OP.mult), ev=True)
    b.wait("A", prev_spill[0])
    b.wait("A", prev_spill[1])
    ldci = b.dma("A", "ccin", cc_in, gg[:], free=evgg)
    b.wait("G", ldci)
    evar = b.op("G", lambda h: h.collective_compute(
        "AllReduce", OP.add, replica_groups=[list(range(NCORES))],
        ins=[cc_in], outs=[cc_out]), ev=True)
    b.wait("A", evar)
    ldco = b.dma("A", "ccout", gg[:], cc_out)
    b.wait("V", ldco)
    evoff = b.op("V", lambda h: h.tensor_scalar(
        out=off_t[:], in0=gg[:, 0:TOTC], scalar1=msel_t[:, 0:1], scalar2=None,
        op0=OP.mult), ev=True)
    for j in range(1, NCORES):
        evoff = b.op("V", lambda h, j=j: h.scalar_tensor_tensor(
            out=off_t[:], in0=gg[:, j * TOTC:(j + 1) * TOTC], scalar=msel_t[:, j:j + 1],
            in1=off_t[:], op0=OP.mult, op1=OP.add), ev=True)
    if debug == "c4t":
        b.barrier()
        b.op("V", lambda h: h.tensor_copy(out=gg[:, 0:TOTC], in_=tot_t[:]))
        evd = b.op("V", lambda h: h.tensor_copy(out=gg[:, TOTC:2 * TOTC], in_=off_t[:]),
                   ev=True)
        b.wait("S", evd)
        lo = b.dma("S", "dbgo", dbgD[:], gg[:, 0:2 * TOTC])
        lo2 = b.dma("S", "dbgo2", dbgC, mrg_dram)
        b.wait("G", lo)
        b.wait("G", lo2)
        b.op("G", lambda h: h.engine_nop())
        b.emit(block)
        close_all()
        es.close()
        return nc
    closestack(stCC)
    b.barrier()
    closestack(stC4)

    b.phase = "C6"
    stC6 = mkstack()
    sum8_t = stC6.enter_context(nc.sbuf_tensor("sum8_t", [128, NCH * 256], BF16))
    wblk_t = stC6.enter_context(nc.sbuf_tensor("wblk_t", [128, 4 * 1536], BF16))
    wout_t = stC6.enter_context(nc.sbuf_tensor("wout_t", [128, 6 * D], BF16))
    normc_t = stC6.enter_context(nc.sbuf_tensor("normc_t", [128, 2], F32))
    repden_t = stC6.enter_context(nc.sbuf_tensor("repden_t", [32, 256], F32))
    wstg6 = stC6.enter_context(nc.sbuf_tensor("wstg6", [128, 1024], F32))
    csre = stC6.enter_context(nc.sbuf_tensor("csre", [128, 512], F32))
    csim = stC6.enter_context(nc.sbuf_tensor("csim", [128, 512], F32))
    if debug == "c6s":
        snapR = stC6.enter_context(nc.sbuf_tensor("snapR", [128, 512], F32))
        snapI = stC6.enter_context(nc.sbuf_tensor("snapI", [128, 512], F32))
        snap2 = stC6.enter_context(nc.sbuf_tensor("snap2", [128, 64], F32))
    stRI = stC6.enter_context(nc.sbuf_tensor("stRI", [128, 2 * NCH], F32))
    mgl = [stC6.enter_context(nc.sbuf_tensor(f"mgl{i}", [128, 512], BF16)) for i in range(4)]
    qrb = [stC6.enter_context(nc.sbuf_tensor(f"qrb{i}", [128, 128], BF16)) for i in range(4)]
    qreb = stC6.enter_context(nc.sbuf_tensor("qreb", [128, 512], BF16))
    qimb = stC6.enter_context(nc.sbuf_tensor("qimb", [128, 512], BF16))
    prod = [stC6.enter_context(nc.sbuf_tensor(f"prod{i}", [128, 512], BF16)) for i in range(4)]
    dcs = stC6.enter_context(nc.sbuf_tensor("dcs", [32, 512], F32))
    dinv = stC6.enter_context(nc.sbuf_tensor("dinv", [32, 512], F32))
    denb = stC6.enter_context(nc.sbuf_tensor("denb", [128, 2 * 512], F32))
    outcb = stC6.enter_context(nc.sbuf_tensor("outcb", [128, 4 * 512], BF16))
    silg = stC6.enter_context(nc.sbuf_tensor("silg", [128, 512], BF16))
    yh = stC6.enter_context(nc.sbuf_tensor("yh", [128, 6 * 512], BF16))

    ldc6 = None
    b.wait("S", evar)
    for ci in range(NCH):
        ldc6 = b.dma("S", "cc6", sum8_t[:, ci * 256:(ci + 1) * 256], sum8_in[ci])
    ldc6 = b.dma("S", "cc6", repden_t[:], repden_in)
    ldc6 = b.dma("S", "cc6", normc_t[:], normc_in)
    wsB = {"c0": wsM.get("c0"), "c1": wsM.get("c1")}
    piecesB = []
    for j in range(2):
        for src_, doff in [(wblkre_in[j], j * 1536), (wblkim_in[j], (2 + j) * 1536)]:
            for q_ in range(3):
                piecesB.append((src_[:, q_ * 512:(q_ + 1) * 512],
                                wblk_t[:, doff + q_ * 512:doff + (q_ + 1) * 512]))
    for yc in range(6):
        for q_ in range(4):
            piecesB.append((woutE[yc][:, q_ * 512:(q_ + 1) * 512],
                            wout_t[:, yc * D + q_ * 512:yc * D + (q_ + 1) * 512]))
    for j, (src_, dst_) in enumerate(piecesB):
        r = j % 2
        ld = b.dma("S", f"wS{r}", wstg6[:, r * 512:(r + 1) * 512], src_,
                   free=wsB.get(f"c{r}"))
        b.wait("G", ld)
        wsB[f"c{r}"] = b.op("G", lambda h, src_=src_, dst_=dst_, r=r: h.tensor_copy(
            out=dst_, in_=wstg6[:, r * 512:(r + 1) * 512]), ev=True)
    wsB["conv"] = wsB["c1"]
    ev_wc6 = wsB["c1"]

    y3_ev = {}
    mgl_free = [None] * 4
    qrb_free = [None] * 4
    b.wait("V", evoff)
    b.op("V", lambda h: h.tensor_tensor(
        out=stRI[:], in0=off_t[:, 0:2 * NCH], in1=tot0_t[:], op=OP.add))
    for s in range(2):
        ls, lw = LKT[s]
        ev_scan = {}
        for ci in range(NCH):
            sl0, sl1 = (2 * ci) % 4, (2 * ci + 1) % 4
            lm0 = b.dma("S", f"ml{sl0}", mgl[sl0][:, 0:lw], mrg_dram[0, ci, s],
                        free=mgl_free[sl0])
            lm1 = b.dma("S", f"ml{sl1}", mgl[sl1][:, 0:lw], mrg_dram[1, ci, s],
                        free=mgl_free[sl1])
            qs0, qs1 = (2 * ci) % 4, (2 * ci + 1) % 4
            lq0 = b.dma("S", f"ql{qs0}", qrb[qs0][:],
                        repqre_in[ci // 4, :, ci * 128:(ci + 1) * 128],
                        free=qrb_free[qs0])
            lq1 = b.dma("S", f"ql{qs1}", qrb[qs1][:],
                        repqim_in[ci // 4, :, ci * 128:(ci + 1) * 128],
                        free=qrb_free[qs1])
            if s == 0:
                b.wait("V", lm0)
                b.op("V", lambda h, ci=ci, lw=lw, sl0=sl0: h.tensor_tensor_scan(
                    out=csre[:, 0:lw], data0=onest_f[:, 0:lw], data1=mgl[sl0][:, 0:lw],
                    initial=off_t[:, ci:ci + 1], op0=OP.mult, op1=OP.add))
                b.wait("V", lm1)
                ev_scan[ci] = b.op("V", lambda h, ci=ci, lw=lw, sl1=sl1: h.tensor_tensor_scan(
                    out=csim[:, 0:lw], data0=onest_f[:, 0:lw], data1=mgl[sl1][:, 0:lw],
                    initial=off_t[:, NCH + ci:NCH + ci + 1], op0=OP.mult, op1=OP.add), ev=True)
            else:
                b.wait("V", lm0)
                b.op("V", lambda h, ci=ci, lw=lw, sl0=sl0: h.tensor_tensor_scan(
                    out=csre[:, 0:lw], data0=onest_f[:, 0:lw], data1=mgl[sl0][:, 0:lw],
                    initial=stRI[:, ci:ci + 1], op0=OP.mult, op1=OP.add))
                b.wait("V", lm1)
                ev_scan[ci] = b.op("V", lambda h, ci=ci, lw=lw, sl1=sl1: h.tensor_tensor_scan(
                    out=csim[:, 0:lw], data0=onest_f[:, 0:lw], data1=mgl[sl1][:, 0:lw],
                    initial=stRI[:, NCH + ci:NCH + ci + 1], op0=OP.mult, op1=OP.add), ev=True)
            mgl_free[sl0] = ev_scan[ci]
            mgl_free[sl1] = ev_scan[ci]
            if debug == "c6s" and s == 1 and ci == 0:
                b.op("V", lambda h: h.tensor_copy(out=snapR[:], in_=csre[:, 0:512]))
                b.op("V", lambda h: h.tensor_copy(out=snapI[:], in_=csim[:, 0:512]))
            pkr = bank("T", 4)
            b.wait("T", lq0)
            if ci == 0 and s == 0:
                b.wait("T", ev_qp)
            ev_qr = b.op("T", lambda h, lw=lw, ls=ls, ci=ci, pkr=pkr, qs0=qs0: h.matmul(
                out=pkr[:, 0:lw], lhsT=qrb[qs0][:],
                rhs=q_all[:, (ci // 4) * NK + ls:(ci // 4) * NK + ls + lw],
                start=True, stop=True), ev=True)
            pki = bank("T", 5)
            b.wait("T", lq1)
            ev_qi = b.op("T", lambda h, lw=lw, ls=ls, ci=ci, pki=pki, qs1=qs1: h.matmul(
                out=pki[:, 0:lw], lhsT=qrb[qs1][:],
                rhs=q_all[:, (ci // 4) * NK + ls:(ci // 4) * NK + ls + lw],
                start=True, stop=True), ev=True)
            b.wait("V", ev_qr)
            evq0 = b.op("V", lambda h, lw=lw, pkr=pkr: h.tensor_copy(
                out=qreb[:, 0:lw], in_=pkr[:, 0:lw]), ev=True)
            bfree[4] = evq0
            qrb_free[qs0] = ev_qr
            b.wait("V", ev_qi)
            evq1 = b.op("V", lambda h, lw=lw, pki=pki: h.tensor_copy(
                out=qimb[:, 0:lw], in_=pki[:, 0:lw]), ev=True)
            bfree[5] = evq1
            qrb_free[qs1] = ev_qi
            b.op("V", lambda h, lw=lw: h.scalar_tensor_tensor(
                out=prod[0][:, 0:lw], in0=qreb[:, 0:lw], scalar=1.0, in1=csre[:, 0:lw],
                op0=OP.mult, op1=OP.mult))
            b.op("V", lambda h, lw=lw: h.scalar_tensor_tensor(
                out=prod[1][:, 0:lw], in0=qimb[:, 0:lw], scalar=1.0, in1=csim[:, 0:lw],
                op0=OP.mult, op1=OP.mult))
            b.op("V", lambda h, lw=lw: h.scalar_tensor_tensor(
                out=prod[2][:, 0:lw], in0=qreb[:, 0:lw], scalar=1.0, in1=csim[:, 0:lw],
                op0=OP.mult, op1=OP.mult))
            evp = b.op("V", lambda h, lw=lw: h.scalar_tensor_tensor(
                out=prod[3][:, 0:lw], in0=qimb[:, 0:lw], scalar=-1.0, in1=csre[:, 0:lw],
                op0=OP.mult, op1=OP.mult), ev=True)
            j = ci // 8
            b.wait("T", evp)
            if ci % 8 == 0:
                bank("T", j)
                bank("T", 2 + j)
            b.op("T", lambda h, ci=ci, j=j, lw=lw: h.matmul(
                out=pb[j][:, 0:lw], lhsT=sum8_t[:, ci * 256 + j * 128:ci * 256 + (j + 1) * 128],
                rhs=prod[0][:, 0:lw], start=(ci % 8 == 0), stop=False))
            b.op("T", lambda h, ci=ci, j=j, lw=lw: h.matmul(
                out=pb[j][:, 0:lw], lhsT=sum8_t[:, ci * 256 + j * 128:ci * 256 + (j + 1) * 128],
                rhs=prod[1][:, 0:lw], start=False, stop=(ci % 8 == 7)))
            b.op("T", lambda h, ci=ci, j=j, lw=lw: h.matmul(
                out=pb[2 + j][:, 0:lw], lhsT=sum8_t[:, ci * 256 + j * 128:ci * 256 + (j + 1) * 128],
                rhs=prod[2][:, 0:lw], start=(ci % 8 == 0), stop=False))
            evms = b.op("T", lambda h, ci=ci, j=j, lw=lw: h.matmul(
                out=pb[2 + j][:, 0:lw], lhsT=sum8_t[:, ci * 256 + j * 128:ci * 256 + (j + 1) * 128],
                rhs=prod[3][:, 0:lw], start=False, stop=(ci % 8 == 7)), ev=True)
            if ci == 7:
                ev_msum0 = evms
        ev_msum1 = evms
        if s == 0:
            b.op("V", lambda h, lw=lw: h.tensor_tensor_scan(
                out=dcs[:, 0:lw], data0=onest_f[0:32, 0:lw], data1=pw_t[0:32, 0:lw],
                initial=off_t[0:32, 32:33], op0=OP.mult, op1=OP.add))
        else:
            b.op("V", lambda h, ls=ls, lw=lw: h.tensor_tensor_scan(
                out=dcs[:, 0:lw], data0=onest_f[0:32, 0:lw], data1=pw_t[0:32, ls:ls + lw],
                initial=dcs[:, 511:512], op0=OP.mult, op1=OP.add))
        b.op("V", lambda h, lw=lw: h.tensor_scalar(
            out=dinv[:, 0:lw], in0=dcs[:, 0:lw], scalar1=1e-4, scalar2=None, op0=OP.max))
        evdi = b.op("V", lambda h, lw=lw: h.reciprocal(out=dinv[:, 0:lw], in_=dinv[:, 0:lw]),
                    ev=True)
        b.wait("T", evdi)
        for j in range(2):
            pkd = bank("T", 4 + j)
            ev_dr = b.op("T", lambda h, j=j, lw=lw, pkd=pkd: h.matmul(
                out=pkd[:, 0:lw], lhsT=repden_t[0:32, j * 128:(j + 1) * 128],
                rhs=dinv[0:32, 0:lw], start=True, stop=True), ev=True)
            b.wait("V", ev_dr)
            evde = b.op("V", lambda h, j=j, lw=lw, pkd=pkd: h.tensor_copy(
                out=denb[:, j * 512:j * 512 + lw], in_=pkd[:, 0:lw]), ev=True)
            bfree[4 + j] = evde
        b.wait("V", ev_msum0 if s == 0 else ev_msum1)
        b.wait("V", ev_msum1)
        evoc = None
        for j in range(2):
            b.op("V", lambda h, j=j, lw=lw: h.scalar_tensor_tensor(
                out=outcb[:, j * 512:j * 512 + lw], in0=pb[j][:, 0:lw],
                scalar=normc_t[:, j:j + 1], in1=denb[:, j * 512:j * 512 + lw],
                op0=OP.mult, op1=OP.mult))
            evoc = b.op("V", lambda h, j=j, lw=lw: h.scalar_tensor_tensor(
                out=outcb[:, (2 + j) * 512:(2 + j) * 512 + lw], in0=pb[2 + j][:, 0:lw],
                scalar=normc_t[:, j:j + 1], in1=denb[:, j * 512:j * 512 + lw],
                op0=OP.mult, op1=OP.mult), ev=True)
        for j in range(4):
            bfree[j] = evoc
        b.wait("T", evoc)
        if s == 0:
            b.wait("T", ev_wc6)
            b.wait("T", ldc6)
        for wc in range(12):
            pkw = bank("T", 6 + wc % 2)
            def mmw(h, wc=wc, lw=lw, pkw=pkw):
                ins = None
                for j in range(2):
                    h.matmul(out=pkw[:, 0:lw],
                             lhsT=wblk_t[:, j * 1536 + wc * 128:j * 1536 + (wc + 1) * 128],
                             rhs=outcb[:, j * 512:j * 512 + lw], start=(j == 0), stop=False)
                    ins = h.matmul(out=pkw[:, 0:lw],
                                   lhsT=wblk_t[:, (2 + j) * 1536 + wc * 128:(2 + j) * 1536 + (wc + 1) * 128],
                                   rhs=outcb[:, (2 + j) * 512:(2 + j) * 512 + lw],
                                   start=False, stop=(j == 1))
                return ins
            evw = b.op("T", mmw, ev=True)
            if wc < 6:
                b.wait("V", evw)
                evh = b.op("V", lambda h, wc=wc, lw=lw, pkw=pkw: h.tensor_copy(
                    out=yh[:, wc * 512:wc * 512 + lw], in_=pkw[:, 0:lw]), ev=True)
                bfree[6 + wc % 2] = evh
            else:
                b.wait("A", evw)
                evg = b.op("A", lambda h, lw=lw, pkw=pkw: h.activation(
                    out=silg[:, 0:lw], in_=pkw[:, 0:lw], func=AF.Silu), ev=True)
                bfree[6 + wc % 2] = evg
                b.wait("V", evg)
                evh = b.op("V", lambda h, wc=wc, lw=lw: h.tensor_tensor(
                    out=yh[:, (wc - 6) * 512:(wc - 6) * 512 + lw],
                    in0=yh[:, (wc - 6) * 512:(wc - 6) * 512 + lw],
                    in1=silg[:, 0:lw], op=OP.mult), ev=True)
        ev_yh = evh
        b.wait("T", ev_yh)
        for m in range(NCH):
            pko = bank("T", m % 2)
            def mmo2(h, m=m, lw=lw, pko=pko):
                ins = None
                for yc in range(6):
                    ins = h.matmul(out=pko[:, 0:lw],
                                   lhsT=wout_t[:, yc * D + m * 128:yc * D + (m + 1) * 128],
                                   rhs=yh[:, yc * 512:yc * 512 + lw],
                                   start=(yc == 0), stop=(yc == 5))
                return ins
            evm2 = b.op("T", mmo2, ev=True)
            b.wait("V", evm2)
            y3_ev[(s, m)] = b.op("V", lambda h, m=m, ls=ls, lw=lw, pko=pko: h.tensor_tensor(
                out=Y1[:, m * KEPT + ls:m * KEPT + ls + lw],
                in0=Y1[:, m * KEPT + HALO + ls:m * KEPT + HALO + ls + lw],
                in1=pko[:, 0:lw], op=OP.add), ev=True)
            bfree[m % 2] = y3_ev[(s, m)]
    if debug == "c6s":
        b.barrier()
        lo = b.dma("S", "dbgo", dbgE[:, 0:32], stRI[:])
        lo = b.dma("S", "dbgo", dbgE[:, 32:544], snapR[:])
        lo = b.dma("S", "dbgo", dbgE[:, 544:1056], snapI[:])
        lo = b.dma("S", "dbgo", dbgE[0:32, 1056:1568], dcs[:])
        lo = b.dma("S", "dbgo", dbgE[:, 1568:2592], denb[:])
        lo = b.dma("S", "dbgo", dbgE[:, 2592:2624], snap2[:, 0:32])
        lo = b.dma("S", "dbgo", dbgE[:, 2624:2656], snap2[:, 32:64])
        lo2 = b.dma("S", "dbgo2", dbgE16[:, 0:2048], outcb[:])
        lo2 = b.dma("S", "dbgo2", dbgE16[:, 2048:5120], yh[:])
        lo2 = b.dma("S", "dbgo2", dbgE16[:, 5120:9216], q_all[:])
        b.wait("G", lo)
        b.wait("G", lo2)
        b.op("G", lambda h: h.engine_nop())
        b.emit(block)
        close_all()
        es.close()
        return nc
    b.barrier()
    closestack(stC6)
    closestack(stC)

    b.phase = "C7"
    LKT3 = [(0, 512, 0), (512, 512, 512)]
    ev_x3 = emit_ln2(lambda m: Y1[:, m * KEPT:m * KEPT + NK], NK, LKT3, LNB)
    lx3 = None
    b.wait("S", ev_x3)
    for m in range(NCH):
        lx3 = b.dma("S", "x3s", x3_dram[m], Y1[:, m * KEPT:m * KEPT + NK])

    if debug == "x3":
        b.wait("S", ev_x3)
        lo2 = None
        for m in range(NCH):
            lo2 = b.dma("S", "outo", out[m * 128:(m + 1) * 128, :],
                        Y1[:, m * KEPT:m * KEPT + NK])
        b.wait("G", lo2)
        b.wait("G", lx3)
        b.op("G", lambda h: h.engine_nop())
        b.emit(block)
        close_all()
        es.close()
        return nc

    b.phase = "D"
    b.barrier()
    closestack(stY)
    stD = mkstack()
    x3b = stD.enter_context(nc.sbuf_tensor("x3b", [128, NCH * NK], BF16))
    h_all = stD.enter_context(nc.sbuf_tensor("h_all", [128, NHC * 512], BF16))
    y4 = stD.enter_context(nc.sbuf_tensor("y4", [128, NCH * 512], F32))
    wist = [stD.enter_context(nc.sbuf_tensor(f"wist{i}", [128, D], U8)) for i in range(2)]
    wbfG = [stD.enter_context(nc.sbuf_tensor(f"wbfG{i}", [128, D], BF16)) for i in range(2)]
    wbfU = [stD.enter_context(nc.sbuf_tensor(f"wbfU{i}", [128, D], BF16)) for i in range(2)]
    wdst = [stD.enter_context(nc.sbuf_tensor(f"wdst{i}", [128, 2816], U8)) for i in range(2)]
    wbfD = [stD.enter_context(nc.sbuf_tensor(f"wbfD{i}", [128, HID], BF16)) for i in range(2)]
    sgu_t = stD.enter_context(nc.sbuf_tensor("sgu_t", [128, 2 * NHC + NCH], F32))
    silt = stD.enter_context(nc.sbuf_tensor("silt", [128, 512], F32))
    upt = stD.enter_context(nc.sbuf_tensor("upt", [128, 512], F32))
    x3r = [stD.enter_context(nc.sbuf_tensor(f"x3r{i}", [128, 512], F32)) for i in range(2)]

    lds = b.dma("S", "cc6", sgu_t[:], sgu_in)
    b.wait("S", lx3)
    x3b_ev = None
    for e in range(NCH):
        ld = b.dma("S", f"x3l{e % 2}", x3r[e % 2][:, 0:512], x3_dram[e, :, 0:512],
                   free=x3b_ev)
        b.wait("V", ld)
        evh1 = b.op("V", lambda h, e=e: h.tensor_copy(out=x3b[:, e * NK:e * NK + 512],
                                                      in_=x3r[e % 2][:, 0:512]), ev=True)
        ld2 = b.dma("S", f"x3m{e % 2}", x3r[e % 2][:, 0:512], x3_dram[e, :, 512:1024],
                    free=evh1)
        b.wait("V", ld2)
        x3b_ev = b.op("V", lambda h, e=e: h.tensor_copy(
            out=x3b[:, e * NK + 512:e * NK + 1024], in_=x3r[e % 2][:, 0:512]), ev=True)
    b.wait("V", lds)

    out_ev = None
    cvG = {}
    cvU = {}
    evG = {}
    evU = {}
    evSG = {}
    evUP = {}
    cvD = {}
    evD = {}
    for s in range(2):
        ls, lw = LKT[s]
        h_ev = {}
        base = s * NHC

        def emit_gu_casts(hc, s=s, base=base):
            k = base + hc
            ldg = b.dma("S", "wg", wist[0][:], wgE[hc], free=cvG.get(k - 1))
            ldu = b.dma("A", "wu", wist[1][:], wuE[hc], free=cvU.get(k - 1))
            b.wait("V", ldg)
            b.wait("V", evG.get(k - 2))
            cvG[k] = b.op("V", lambda h, hc=hc, k=k: h.tensor_scalar(
                out=wbfG[k % 2][:], in0=wist[0][:], scalar1=-128.0, scalar2=None,
                op0=OP.add), ev=True)
            b.wait("V", ldu)
            b.wait("V", evU.get(k - 2))
            cvU[k] = b.op("V", lambda h, hc=hc, k=k: h.tensor_scalar(
                out=wbfU[k % 2][:], in0=wist[1][:], scalar1=-128.0, scalar2=None,
                op0=OP.add), ev=True)

        if s == 0:
            emit_gu_casts(0)
        for hc in range(NHC):
            k = base + hc
            if hc + 1 < NHC:
                emit_gu_casts(hc + 1)
            elif s == 0:
                emit_gu_casts(0, s=1, base=NHC)
            b.wait("T", cvG[k])
            if hc == 0 and s == 0:
                b.wait("T", x3b_ev)
            pg = bank("T", k % 2)
            def mmg(h, k=k, ls=ls, lw=lw, pg=pg):
                ins = None
                for kd in range(NCH):
                    ins = h.matmul(out=pg[:, 0:lw],
                                   lhsT=wbfG[k % 2][:, kd * 128:(kd + 1) * 128],
                                   rhs=x3b[:, kd * NK + ls:kd * NK + ls + lw],
                                   start=(kd == 0), stop=(kd == NCH - 1))
                return ins
            evG[k] = b.op("T", mmg, ev=True)
            b.wait("T", cvU[k])
            pu = bank("T", 2 + k % 2)
            def mmu(h, k=k, ls=ls, lw=lw, pu=pu):
                ins = None
                for kd in range(NCH):
                    ins = h.matmul(out=pu[:, 0:lw],
                                   lhsT=wbfU[k % 2][:, kd * 128:(kd + 1) * 128],
                                   rhs=x3b[:, kd * NK + ls:kd * NK + ls + lw],
                                   start=(kd == 0), stop=(kd == NCH - 1))
                return ins
            evU[k] = b.op("T", mmu, ev=True)
            b.wait("A", evG[k])
            b.wait("A", h_ev.get(hc - 1))
            evSG[k] = b.op("A", lambda h, hc=hc, lw=lw, pg=pg: h.activation(
                out=silt[:, 0:lw], in_=pg[:, 0:lw], func=AF.Silu,
                scale=sgu_t[:, hc:hc + 1]), ev=True)
            bfree[k % 2] = evSG[k]
            b.wait("V", evU[k])
            evUP[k] = b.op("V", lambda h, hc=hc, lw=lw, pu=pu: h.tensor_scalar(
                out=upt[:, 0:lw], in0=pu[:, 0:lw], scalar1=sgu_t[:, NHC + hc:NHC + hc + 1],
                scalar2=None, op0=OP.mult), ev=True)
            bfree[2 + k % 2] = evUP[k]
            b.wait("V", evSG[k])
            h_ev[hc] = b.op("V", lambda h, hc=hc, lw=lw: h.tensor_tensor(
                out=h_all[:, hc * 512:hc * 512 + lw], in0=silt[:, 0:lw], in1=upt[:, 0:lw],
                op=OP.mult), ev=True)
        ev_h = h_ev[NHC - 1]

        y4_ev = {}
        ldr = {}

        def emit_d_casts(m, s=s):
            k = s * NCH + m
            for half in range(2):
                ldd = b.dma("A", "wd", wdst[half][:],
                            wdE[m, :, half * 2816:(half + 1) * 2816],
                            free=cvD.get((k - 1, half)))
                b.wait("V", ldd)
                if half == 0:
                    b.wait("V", evD.get(k - 2))
                cvD[(k, half)] = b.op("V", lambda h, k=k, half=half: h.tensor_scalar(
                    out=wbfD[k % 2][:, half * 2816:(half + 1) * 2816], in0=wdst[half][:],
                    scalar1=-128.0, scalar2=None, op0=OP.add), ev=True)

        if s == 0:
            emit_d_casts(0)
        for m in range(NCH):
            k = s * NCH + m
            if m + 1 < NCH:
                emit_d_casts(m + 1)
            elif s == 0:
                emit_d_casts(0, s=1)
            ldr[m] = b.dma("S", f"x3l{m % 2}", x3r[m % 2][:, 0:lw],
                           x3_dram[m, :, ls:ls + lw], free=y4_ev.get(m - 2))
            b.wait("T", cvD[(k, 1)])
            if m == 0:
                b.wait("T", ev_h)
            pd = bank("T", 4 + m % 2)
            def mmd(h, k=k, lw=lw, pd=pd):
                ins = None
                for hc in range(NHC):
                    ins = h.matmul(out=pd[:, 0:lw],
                                   lhsT=wbfD[k % 2][:, hc * 128:(hc + 1) * 128],
                                   rhs=h_all[:, hc * 512:hc * 512 + lw],
                                   start=(hc == 0), stop=(hc == NHC - 1))
                return ins
            evD[k] = b.op("T", mmd, ev=True)
            b.wait("V", evD[k])
            b.wait("V", ldr[m])
            b.op("V", lambda h, m=m, lw=lw, pd=pd: h.tensor_scalar(
                out=silt[:, 0:lw], in0=pd[:, 0:lw],
                scalar1=sgu_t[:, 2 * NHC + m:2 * NHC + m + 1], scalar2=None, op0=OP.mult))
            y4_ev[m] = b.op("V", lambda h, m=m, lw=lw: h.tensor_tensor(
                out=y4[:, m * 512:m * 512 + lw], in0=x3r[m % 2][:, 0:lw],
                in1=silt[:, 0:lw], op=OP.add), ev=True)
            bfree[4 + m % 2] = y4_ev[m]
        b.barrier()
        LKT4 = [(0, lw, 0)]
        ev_x4 = emit_ln2(lambda m: y4[:, m * 512:(m + 1) * 512], 512, LKT4, LNB)
        b.wait("S", ev_x4)
        for m in range(NCH):
            out_ev = b.dma("S", "outo", out[m * 128:(m + 1) * 128, ls:ls + lw],
                           y4[:, m * 512:m * 512 + lw])
        b.barrier()
        b.wait("V", out_ev)
    b.wait("G", out_ev)
    b.op("G", lambda h: h.engine_nop())

    b.emit(block)
    close_all()
    es.close()
    return nc


BFH = ml_dtypes.bfloat16
F8IEEE = ml_dtypes.float8_e4m3


def _emajor(W):
    E, Din = W.shape
    ne, nk = E // 128, Din // 128
    return np.ascontiguousarray(
        W.reshape(ne, 128, nk, 128).transpose(0, 3, 2, 1).reshape(ne, 128, nk * 128))


def _cols(v, n=None):
    v = np.asarray(v).reshape(-1)
    n = n if n is not None else v.size // 128
    return np.ascontiguousarray(v.reshape(n, 128).T.astype(np.float32))


def _make_shared(d):
    sh = {}
    sh["wkE"] = _emajor(np.asarray(d["L0_attn_Wk"], np.float32))
    sh["wvE"] = _emajor(np.asarray(d["L0_attn_Wv"], np.float32))
    sh["wrE"] = _emajor(np.asarray(d["L0_attn_Wr"], np.float32))
    sh["woE"] = _emajor(np.asarray(d["L0_attn_Wo"], np.float32))
    w8b = np.asarray(d["L0_ffn_weight"]).view(np.uint8)
    f8 = w8b.reshape(16, 128, 16, 128).transpose(0, 3, 2, 1).reshape(16, 128, 2048)
    sh["f8E"] = np.ascontiguousarray(f8).view(F8IEEE)
    sh["tm_in"] = np.concatenate([_cols(d["L0_attn_time_mix_k"]),
                                  _cols(d["L0_attn_time_mix_v"]),
                                  _cols(d["L0_attn_time_mix_r"])], axis=1)
    sh["td_in"] = _cols(d["L0_attn_time_decay"])
    sh["tf_in"] = _cols(d["L0_attn_time_first"])
    s_f32 = np.float32(np.asarray(d["L0_ffn_input_scale"]).astype(np.float32))
    sh["scl_in"] = np.array([[s_f32, np.float32(d["L0_ffn_weight_scale"])]], np.float32)
    cc = np.zeros((128, NCONST), np.float32)
    cc[:, 0] = PI / 2
    cc[:, 1] = 1.0 / (2 * PI)
    cc[:, 2] = 1e-5
    cc[:, 3] = -1.0
    cc[:, 4] = 1e-30
    cc[:, 5] = 1.0
    cc[:, 6] = -0.5
    sh["consts"] = cc
    sh["ones_bf_in"] = np.ones((1, 128), BFH)
    sh["ones_f_in"] = np.ones((1, 128), np.float32)
    sh["ones_col_in"] = np.ones((128, 1), BFH)
    sh["onesT_in"] = np.ones((128, 512), np.float32)
    Wm = np.zeros((2048, 384), np.float32)
    Wm[:, :288] = np.asarray(d["L1_attn_W_mem"], np.float32)
    wmem = np.zeros((128, NCH * 384), np.float32)
    for kd in range(NCH):
        wmem[:, kd * 384:(kd + 1) * 384] = Wm[kd * 128:(kd + 1) * 128, :]
    sh["wmem_in"] = wmem
    Wq = np.asarray(d["L1_attn_W_q"], np.float32)
    wqp = np.zeros((128, NCH * 512), np.float32)
    for kd in range(NCH):
        wqp[:, kd * 512:(kd + 1) * 512] = Wq[kd * 128:(kd + 1) * 128, :]
    sh["wqp_in"] = wqp
    ckk = np.asarray(d["L1_attn_conv_k"], np.float32)
    ck = np.zeros((128, 12), np.float32)
    for j in range(4):
        for zc in range(3):
            n = 128 if zc < 2 else 32
            ck[:n, j * 3 + zc] = ckk[j, 0, zc * 128:zc * 128 + n]
    sh["ck_in"] = ck
    theta = np.asarray(d["L1_attn_theta"], np.float32).reshape(32, 8, 8)
    th = np.zeros((128, NCH), np.float32)
    for ci in range(NCH):
        for p in range(128):
            th[p, ci] = theta[2 * ci + (1 if p >= 64 else 0), (p % 64) // 8, p % 8]
    sh["theta_in"] = th
    sh["sslop_in"] = np.concatenate([np.asarray(d["L1_attn_decay_slopes"], np.float32),
                                     np.asarray(d["L1_attn_anchor_slopes"], np.float32)]
                                    ).reshape(32, 1)
    sh["ssc_in"] = np.asarray(d["L1_attn_score_scale"], np.float32).reshape(32, 1)
    pgrid = np.arange(128)
    kofp = lambda ci: 2 * ci + (pgrid >= 64).astype(np.int64)
    hofp = (pgrid % 64) // 8
    mofp = pgrid % 8
    repkh = np.zeros((2, 128, FLAT), np.float32)
    repk = np.zeros((32, FLAT), np.float32)
    for ci in range(NCH):
        kk = kofp(ci)
        khrow = kk * 8 + hofp
        kchunk = ci // 8
        for pc in range(128):
            repkh[kchunk, khrow[pc] - kchunk * 128, ci * 128 + pc] = 1.0
            repk[kk[pc], ci * 128 + pc] = 1.0
    sh["repkh_in"] = repkh
    sh["repk_in"] = repk
    repden = np.zeros((32, 256), np.float32)
    for c in range(256):
        repden[c // 8, c] = 1.0
    sh["repden_in"] = repden
    repqre = np.zeros((4, 128, FLAT), BFH)
    repqim = np.zeros((4, 128, FLAT), BFH)
    for ci in range(NCH):
        qc = ci // 4
        kk = kofp(ci)
        kq = kk // 8
        for pc in range(128):
            qrow_re = ((kq[pc] * 8 + hofp[pc]) * 8 + mofp[pc]) * 2
            repqre[qc, qrow_re - qc * 128, ci * 128 + pc] = 1.0
            repqim[qc, qrow_re + 1 - qc * 128, ci * 128 + pc] = 1.0
    sh["repqre_in"] = repqre
    sh["repqim_in"] = repqim
    sum8 = np.zeros((NCH, 128, 256), BFH)
    for ci in range(NCH):
        kk = kofp(ci)
        for p in range(128):
            sum8[ci, p, kk[p] * 8 + hofp[p]] = 1.0
    sh["sum8_in"] = sum8
    Wre = np.asarray(d["L1_attn_W_re"], np.float32)
    Wim = np.asarray(d["L1_attn_W_im"], np.float32)
    wblkre = np.zeros((2, 128, 1536), np.float32)
    wblkim = np.zeros((2, 128, 1536), np.float32)
    for kh in range(256):
        k, h = kh // 8, kh % 8
        j, p = kh // 128, kh % 128
        for n in range(48):
            g, npp = (0, n) if n < 24 else (1, n - 24)
            col = g * 768 + k * 24 + npp
            wblkre[j, p, col] = Wre[k, h, n]
            wblkim[j, p, col] = Wim[k, h, n]
    sh["wblkre_in"] = wblkre
    sh["wblkim_in"] = wblkim
    ns = np.asarray(d["L1_attn_norm_scale"], np.float32).reshape(256)
    sh["normc_in"] = np.ascontiguousarray(ns.reshape(2, 128).T)
    sh["woutE"] = np.ascontiguousarray(
        np.asarray(d["L1_attn_W_out"], np.float32).reshape(6, 128, 2048))
    sh["wgE"] = np.ascontiguousarray(
        np.asarray(d["L1_ffn_wq_gate"]).reshape(NHC, 128, 16, 128)
        .transpose(0, 3, 2, 1).reshape(NHC, 128, 2048)).astype(np.uint8)
    sh["wuE"] = np.ascontiguousarray(
        np.asarray(d["L1_ffn_wq_up"]).reshape(NHC, 128, 16, 128)
        .transpose(0, 3, 2, 1).reshape(NHC, 128, 2048)).astype(np.uint8)
    sh["wdE"] = np.ascontiguousarray(
        np.asarray(d["L1_ffn_wq_down"]).reshape(16, 128, NHC, 128)
        .transpose(0, 3, 2, 1).reshape(16, 128, HID)).astype(np.uint8)
    sh["sgu_in"] = np.concatenate([_cols(d["L1_ffn_s_gate"], NHC),
                                   _cols(d["L1_ffn_s_up"], NHC),
                                   _cols(d["L1_ffn_s_down"], 16)], axis=1)
    return sh


def _make_in_maps(d):
    sh = _make_shared(d)
    x = np.asarray(d["L0_attn_x"], np.float32)[0]
    maps = []
    for c in range(NCORES):
        m = dict(sh)
        start = c * NK
        lo = start - (WARM + HALO + 1)
        rows = np.zeros((XR, D), np.float32)
        src_lo = max(lo, 0)
        rows[src_lo - lo:] = x[src_lo:start + NK]
        m["xT"] = np.ascontiguousarray(rows.T)
        mb = np.zeros((1, SCAN), np.float32)
        if c == 0:
            mb[0, :WARM + HALO] = -1e9
        m["maskb"] = mb.astype(BFH)
        feat = np.zeros((32, NK), np.float32)
        pos = np.arange(start, start + NK, dtype=np.float32)
        feat[:28, :] = np.maximum(np.float32(8191) - pos, 0.0)[None, :]
        feat[28:, :] = pos[None, :]
        m["feat_in"] = feat
        mc = np.zeros((128, NCORES), np.float32)
        mc[:, c + 1:] = 1.0
        m["mcon_in"] = mc
        ms = np.zeros((128, NCORES), np.float32)
        ms[:, c] = 1.0
        m["msel_in"] = ms
        maps.append(m)
    return maps


_BUILT = {}


def kernel(**inputs):
    from concourse.bass_utils import run_bass_kernel_spmd
    if "nc" not in _BUILT:
        _BUILT["nc"] = build()
    maps = _make_in_maps(inputs)
    res = run_bass_kernel_spmd(_BUILT["nc"], maps, core_ids=list(range(NCORES)))
    out = np.empty((1, 8192, 2048), np.float32)
    for c in range(NCORES):
        out[0, c * NK:(c + 1) * NK] = np.asarray(res.results[c]["out"]).T
    return out
